# version 15 (final)
# speedup vs baseline: 1.3290x; 1.0552x over previous
import numpy as np
import ml_dtypes
from contextlib import ExitStack

import concourse.bass as bass
import concourse.mybir as mybir
from concourse.bass_utils import run_bass_kernel_spmd

F32 = mybir.dt.float32
BF16 = mybir.dt.bfloat16
AF = mybir.ActivationFunctionType
ALU = mybir.AluOpType
AX = mybir.AxisListType

S = 2048
DM = 2048
NDC = 16
NST = 16
H = 16
HD = 128
NM = 4
EPS_LN = 1e-5
NE = 8
NIC = 11
NSC = 44
NHC = 40
SEM_CAP = 15000
STRICT_SYNC = False
DEBUG_TAPS = False



def _alibi_slopes(n):
    import math
    def pow2(m):
        start = 2 ** (-2 ** (-(math.log2(m) - 3)))
        return [start * start ** i for i in range(m)]
    if math.log2(n).is_integer():
        return pow2(n)
    c = 2 ** math.floor(math.log2(n))
    return pow2(c) + _alibi_slopes(2 * c)[0::2][:n - c]


def _pack_lhsT(wT, nblk, blkw):
    K, M = wT.shape
    nkc = K // 128
    a = wT.reshape(nkc, 128, nblk, blkw).transpose(2, 1, 0, 3)
    return np.ascontiguousarray(a.reshape(nblk, 128, nkc * blkw))


def _prep_common(inputs):
    f = lambda a: np.ascontiguousarray(np.asarray(a, dtype=np.float32))
    cm = {}
    for nm in ("w_q", "w_k", "w_v"):
        cm["a0_" + nm[2:]] = _pack_lhsT(f(inputs["L0_attn_" + nm]).T, H, HD)
    cm["a0_wo"] = np.ascontiguousarray(f(inputs["L0_attn_w_o"]).T.reshape(H, HD, DM))
    mem = f(inputs["L0_attn_memories"])
    cm["a0_mem"] = np.ascontiguousarray(mem.transpose(1, 2, 0, 3).reshape(H, HD, NM * HD))
    nrm = f(inputs["L0_attn_memory_norms"])
    cm["a0_norms"] = np.ascontiguousarray(nrm.transpose(1, 2, 0))
    cm["a0_normsf"] = np.ascontiguousarray(nrm.reshape(NM * H, HD))
    cm["a0_gate"] = f(inputs["L0_attn_gate"]).reshape(1, H)

    gw = f(inputs["L0_ffn_gate_w"])
    cm["f0_gwT"] = np.ascontiguousarray(gw.T.reshape(NDC, 128, NE).transpose(1, 0, 2)
                                        .reshape(128, NDC * NE))
    cm["f0_wg"] = np.ascontiguousarray(
        f(inputs["L0_ffn_w_gate"]).reshape(NE, NDC, 128, NIC, 128)
        .transpose(0, 3, 2, 1, 4).reshape(NE, NIC, 128, NDC * 128))
    cm["f0_wu"] = np.ascontiguousarray(
        f(inputs["L0_ffn_w_up"]).reshape(NE, NDC, 128, NIC, 128)
        .transpose(0, 3, 2, 1, 4).reshape(NE, NIC, 128, NDC * 128))
    cm["f0_wd"] = np.ascontiguousarray(
        f(inputs["L0_ffn_w_down"]).reshape(NE, NIC, 128, 4, 512)
        .transpose(0, 3, 1, 2, 4))
    cm["f0_swg"] = np.ascontiguousarray(
        f(inputs["L0_ffn_sw_gate"]).reshape(NDC, 128, NSC, 128)
        .transpose(2, 1, 0, 3).reshape(NSC, 128, NDC * 128))
    cm["f0_swu"] = np.ascontiguousarray(
        f(inputs["L0_ffn_sw_up"]).reshape(NDC, 128, NSC, 128)
        .transpose(2, 1, 0, 3).reshape(NSC, 128, NDC * 128))
    cm["f0_swd"] = np.ascontiguousarray(
        f(inputs["L0_ffn_sw_down"]).reshape(NSC, 128, 4, 512)
        .transpose(2, 0, 1, 3))
    cm["f0_sgw"] = np.ascontiguousarray(
        f(inputs["L0_ffn_shared_gate_w"]).reshape(NDC, 128).T)

    for nm in ("Wq", "Wk", "Wv"):
        cm["a1_" + nm[1:].lower()] = _pack_lhsT(f(inputs["L1_attn_" + nm]).T, H, HD)
    cm["a1_wo"] = np.ascontiguousarray(f(inputs["L1_attn_Wo"]).T.reshape(H, HD, DM))

    cm["f1_wfc"] = np.ascontiguousarray(
        f(inputs["L1_ffn_w_fc"]).T.reshape(NDC, 128, NHC, 128)
        .transpose(2, 1, 0, 3).reshape(NHC, 128, NDC * 128))
    cm["f1_wpr"] = np.ascontiguousarray(
        f(inputs["L1_ffn_w_proj"]).T.reshape(NHC, 128, DM))

    slopes = np.array(_alibi_slopes(H), dtype=np.float32)
    ac = np.zeros((128, H * 16), dtype=np.float32)
    for h in range(H):
        for jt in range(16):
            ac[:, h * 16 + jt] = slopes[h] * (jt * 128 + np.arange(128))
    cm["c_alibi_cols"] = ac
    cm["c_negslope"] = np.ascontiguousarray(
        -slopes[:, None] * np.arange(S, dtype=np.float32)[None, :])
    cm["c_trilA"] = (np.arange(128)[:, None] <= np.arange(128)[None, :]
                     ).astype(np.float32)
    cam = np.zeros((128, 4 * 512), dtype=np.float32)
    for r in range(4):
        bad = (128 * r + np.arange(128)[:, None]) > np.arange(512)[None, :]
        cam[:, r * 512:(r + 1) * 512][bad] = -1e30
    cm["c_camask"] = cam
    sel = np.zeros((NM * H, NM), dtype=np.float32)
    sel[np.arange(NM * H), np.arange(NM * H) // H] = 1.0
    cm["c_msel"] = sel
    cm["_slopes"] = slopes
    return cm



ENGS = ("sync", "scalar", "vector", "tensor", "gpsimd")


class _Tok:
    __slots__ = ("op", "handle", "value")

    def __init__(self, op):
        self.op = op
        self.handle = None
        self.value = None


class _Op:
    __slots__ = ("eng", "domain", "fn", "waits", "tok", "need_inc", "step", "risky")

    def __init__(self, eng, domain, fn, step, risky=True):
        self.eng = eng
        self.domain = domain
        self.fn = fn
        self.waits = []
        self.tok = _Tok(self)
        self.need_inc = False
        self.step = step
        self.risky = risky


class _Buf:
    __slots__ = ("name", "writers", "readers")

    def __init__(self, name):
        self.name = name
        self.writers = []
        self.readers = []


class Rec:
    def __init__(self, nc):
        self.nc = nc
        self.ops = []
        self.es = ExitStack()
        self._chan = {}
        self._rings = {}
        self._sem_id = 0
        self._btoks = []
        self._bpend = set()

    def sb(self, name, shape, dtype=F32):
        return self.es.enter_context(self.nc.sbuf_tensor(name, shape, dtype))

    def psum(self, name, shape, dtype=F32):
        return self.es.enter_context(self.nc.psum_tensor(name, shape, dtype))

    def buf(self, name):
        return _Buf(name)

    def op(self, eng, fn, ins=(), outs=(), dma=False, risky=True, outs_accum=()):
        domain = ("dma:" + eng) if dma else ("eng:" + eng)
        o = _Op(eng, domain, fn, 16 if dma else 1, risky=risky)
        if dma:
            o.need_inc = True
            ring = self._rings.setdefault(domain, {"sems": [], "n": 0})
            n = ring["n"]
            ring["n"] = n + 1
            slot = n % self.DMA_RING
            if slot >= len(ring["sems"]):
                ring["sems"].append(self._new_sem(domain.replace(":", "_")))
            h = ring["sems"][slot]
            use = n // self.DMA_RING
            if use > 0:
                gate = _Tok(o)
                gate.handle = h
                gate.value = 16 * use
                o.waits.append(gate)
            o.tok.handle = h
            o.tok.value = 16 * (use + 1)
        if eng in self._bpend:
            self._bpend.discard(eng)
            for t in self._btoks:
                self._want(o, t)
        for b in ins:
            for t in b.writers:
                self._want(o, t)
        for b in outs:
            for t in b.writers:
                self._want(o, t)
            for t in b.readers:
                self._want(o, t)
        for b in outs_accum:
            for t in b.readers:
                self._want(o, t)
        self.ops.append(o)
        for b in ins:
            b.readers.append(o.tok)
        for b in outs:
            b.writers = [o.tok]
            b.readers = []
        for b in outs_accum:
            b.writers.append(o.tok)
        return o.tok

    def _want(self, op, tok):
        if tok.op is None:
            op.waits.append(tok)
            return
        if tok.op is op:
            return
        if ((not STRICT_SYNC) and tok.op.domain == op.domain
                and tok.op.step == 1 and op.step == 1
                and (op.eng == "tensor" or not tok.op.risky)):
            return
        tok.op.need_inc = True
        op.waits.append(tok)

    def barrier(self):
        toks = []
        last = {}
        for o in self.ops:
            if o.step == 1:
                last[o.domain] = o
        for o in last.values():
            o.need_inc = True
            toks.append(o.tok)
        for dom, ring in self._rings.items():
            for slot, h in enumerate(ring["sems"]):
                uses = (ring["n"] - slot + self.DMA_RING - 1) // self.DMA_RING
                if uses > 0:
                    t = _Tok(None)
                    t.handle = h
                    t.value = 16 * uses
                    toks.append(t)
        self._btoks = toks
        self._bpend = set(ENGS)

    def dma(self, eng, out_ap, in_ap, ins=(), outs=(), transpose=False, outs_accum=()):
        if transpose:
            fn = lambda e: e.dma_start_transpose(out_ap, in_ap)
        else:
            fn = lambda e: e.dma_start(out_ap, in_ap)
        return self.op(eng, fn, ins=ins, outs=outs, dma=True, outs_accum=outs_accum)

    def mm(self, out_ap, pairs, ins=(), outs=(), first=True, last=True, f32=False):
        def fn(e):
            r = None
            n = len(pairs)
            for i, (l, rr) in enumerate(pairs):
                r = e.matmul(out_ap, l, rr, start=(first and i == 0),
                             stop=(last and i == n - 1))
            if f32:
                r = e.nop()
            return r
        return self.op("tensor", fn, ins=ins, outs=outs, risky=False)

    def _new_sem(self, tag):
        self._sem_id += 1
        return self.es.enter_context(self.nc.semaphore(f"s{self._sem_id}_{tag}"))

    DMA_RING = 24

    def emit(self):
        for o in self.ops:
            if o.step == 16 or not o.need_inc:
                continue
            h, c = self._chan.get(o.domain, (None, 0))
            if h is None or c + o.step > SEM_CAP:
                h = self._new_sem(o.domain.replace(":", "_"))
                c = 0
            c += o.step
            self._chan[o.domain] = (h, c)
            o.tok.handle = h
            o.tok.value = c
        streams = {e: [] for e in ENGS}
        for o in self.ops:
            streams[o.eng].append(o)
        with self.nc.Block() as block:
            def body(eng_obj, eng_name):
                sat = {}
                for o in streams[eng_name]:
                    for t in o.waits:
                        if t.handle is None or sat.get(id(t.handle), -1) >= t.value:
                            continue
                        eng_obj.wait_ge(t.handle, t.value)
                        sat[id(t.handle)] = t.value
                    ins = o.fn(eng_obj)
                    if o.need_inc:
                        assert ins is not None, "inc op emitted nothing"
                        ins.then_inc(o.tok.handle, o.step)
            for name in ENGS:
                if streams[name]:
                    getattr(block, name)(lambda e, _n=name: body(e, _n))



class Ctx:
    pass


def _setup_common(nc, R, xfeat_dram):
    C = Ctx()
    C.nc = nc
    C.R = R
    C.xT = R.sb("xT_bf", [128, NDC * S], BF16)
    C.xT_b = [R.buf(f"xT{dc}") for dc in range(NDC)]
    C.xTa = lambda dc: C.xT[:, dc * S:(dc + 1) * S]
    C.ones_col_f = R.sb("ones_col_f", [128, 1], F32)
    C.ones_col_b = R.sb("ones_col_b", [128, 1], BF16)
    C.ones_row_f = R.sb("ones_row_f", [1, 128], F32)
    C.ones_row_b = R.sb("ones_row_b", [1, 128], BF16)
    C.eps_col = R.sb("eps_col", [128, 1], F32)
    C.const_b = [R.buf("consts")]
    for t, v in ((C.ones_col_f, 1.0), (C.ones_col_b, 1.0), (C.ones_row_f, 1.0),
                 (C.ones_row_b, 1.0), (C.eps_col, EPS_LN)):
        R.op("gpsimd", lambda e, _t=t, _v=v: e.memset(_t[:, :], _v), outs=C.const_b)
    if xfeat_dram is not None:
        for dc in range(NDC):
            R.dma("gpsimd", C.xTa(dc), xfeat_dram[dc * 128:(dc + 1) * 128, :],
                  outs=[C.xT_b[dc]])
    return C


def _mk_ln_scratch(R, tag, alloc=None, sq=None):
    alloc = alloc or R.sb
    sc = {}
    for nm, shp in (("sum", [128, 1]), ("sumsq", [128, 1]), ("mean", [128, 1]),
                    ("m2", [128, 1]), ("var", [128, 1]), ("lnv", [128, 1]),
                    ("rstd", [128, 1])):
        sc[nm] = alloc(f"{tag}_{nm}", shp, F32)
        sc[nm + "_b"] = R.buf(f"{tag}_{nm}")
    if sq is None:
        sc["sq"] = alloc(f"{tag}_sq", [128, DM], F32)
        sc["sq_b"] = R.buf(f"{tag}_sq")
    else:
        sc["sq"], sc["sq_b"] = sq
    return sc


def _hilo(R, scr, src_ap, src_bufs, p, n):
    R.op("vector", lambda e: e.tensor_copy(scr["hi"][0:p, 0:n], src_ap),
         ins=src_bufs, outs=[scr["hi_b"]])
    R.op("vector", lambda e: e.tensor_tensor(scr["lo_f"][0:p, 0:n], src_ap,
                                             scr["hi"][0:p, 0:n], ALU.subtract),
         ins=src_bufs + [scr["hi_b"]], outs=[scr["lo_f_b"]])
    R.op("vector", lambda e: e.tensor_copy(scr["lo"][0:p, 0:n], scr["lo_f"][0:p, 0:n]),
         ins=[scr["lo_f_b"]], outs=[scr["lo_b"]])


def _mk_hilo(R, alloc, tag, p, n):
    scr = {"hi": alloc(f"{tag}_hi", [p, n], BF16), "lo": alloc(f"{tag}_lo", [p, n], BF16),
           "lo_f": alloc(f"{tag}_lof", [p, n], F32)}
    for k in ("hi", "lo", "lo_f"):
        scr[k + "_b"] = R.buf(f"{tag}_{k}")
    return scr


def _bcast_row(R, C, scr, psum_ap, psum_buf, src_row_ap, src_bufs, n):
    _hilo(R, scr, src_row_ap, src_bufs, 1, n)
    R.mm(psum_ap, [(C.ones_row_b[0:1, :], scr["hi"][0:1, 0:n]),
                   (C.ones_row_b[0:1, :], scr["lo"][0:1, 0:n])],
         ins=[scr["hi_b"], scr["lo_b"]] + C.const_b, outs=[psum_buf])


def _colsum(R, C, scr, psum_ap, psum_buf, col_ap, col_bufs):
    _hilo(R, scr, col_ap, col_bufs, 128, 1)
    R.mm(psum_ap, [(scr["hi"][:, 0:1], C.ones_col_b[:, 0:1]),
                   (scr["lo"][:, 0:1], C.ones_col_b[:, 0:1])],
         ins=[scr["hi_b"], scr["lo_b"]] + C.const_b, outs=[psum_buf])


def _ln_tile(R, C, sc, y_ap, y_buf):
    R.op("vector", lambda e: e.tensor_reduce(sc["sum"][:, 0:1], y_ap, AX.X, ALU.add),
         ins=[y_buf], outs=[sc["sum_b"]])
    R.op("scalar", lambda e: e.activation(sc["sq"][:, :], y_ap, AF.Square,
                                          accum_out=sc["sumsq"][:, 0:1]),
         ins=[y_buf], outs=[sc["sq_b"], sc["sumsq_b"]])
    R.op("vector", lambda e: e.tensor_scalar(sc["mean"][:, 0:1], sc["sum"][:, 0:1],
                                             1.0 / DM, None, ALU.mult),
         ins=[sc["sum_b"]], outs=[sc["mean_b"]])
    R.op("vector", lambda e: e.tensor_tensor(sc["m2"][:, 0:1], sc["mean"][:, 0:1],
                                             sc["mean"][:, 0:1], ALU.mult),
         ins=[sc["mean_b"]], outs=[sc["m2_b"]])
    R.op("vector", lambda e: e.scalar_tensor_tensor(sc["var"][:, 0:1],
                                                    sc["sumsq"][:, 0:1], 1.0 / DM,
                                                    sc["m2"][:, 0:1],
                                                    ALU.mult, ALU.subtract),
         ins=[sc["sumsq_b"], sc["m2_b"]], outs=[sc["var_b"]])
    R.op("scalar", lambda e: e.activation(sc["lnv"][:, 0:1], sc["var"][:, 0:1], AF.Ln,
                                          bias=C.eps_col[:, 0:1]),
         ins=[sc["var_b"]] + C.const_b, outs=[sc["lnv_b"]])
    R.op("scalar", lambda e: e.activation(sc["rstd"][:, 0:1], sc["lnv"][:, 0:1], AF.Exp,
                                          scale=-0.5),
         ins=[sc["lnv_b"]], outs=[sc["rstd_b"]])
    R.op("vector", lambda e: e.tensor_scalar(y_ap, y_ap, sc["mean"][:, 0:1],
                                             sc["rstd"][:, 0:1], ALU.subtract, ALU.mult),
         ins=[y_buf, sc["mean_b"], sc["rstd_b"]], outs=[y_buf])


def _store_tile(R, C, st, y_ap, y_buf, out_dram, last_phase, ybf, ybf_b):
    R.dma("sync", out_dram[st * 128:(st + 1) * 128, :], y_ap, ins=[y_buf])
    if not last_phase:
        R.op("vector", lambda e: e.tensor_copy(ybf[:, :], y_ap),
             ins=[y_buf], outs=[ybf_b], risky=False)
        for dc in range(NDC):
            R.dma("scalar", C.xT[:, dc * S + st * 128: dc * S + (st + 1) * 128],
                  ybf[:, dc * 128:(dc + 1) * 128], ins=[ybf_b],
                  outs_accum=[C.xT_b[dc]], transpose=True)



def emit_phaseD_tern(R, C, dr):
    nc = C.nc
    q2fc = nc.dram_tensor("q2fc_scr", [NHC, 128, DM], BF16, kind="Internal")
    q2pr = nc.dram_tensor("q2pr_scr", [NHC, 128, DM], BF16, kind="Internal")
    q2fc_b = R.buf("q2fc")
    q2pr_b = R.buf("q2pr")

    thr_cols = R.sb("d_thrc", [128, 4])
    thr_cols_b = R.buf("d_thrc")
    csc = R.sb("d_csc", [128, 2])
    csc_b = R.buf("d_csc")

    sco = ExitStack()
    if True:
        sb = lambda nm, shp, dt=F32: sco.enter_context(nc.sbuf_tensor(nm, shp, dt))
        ps = lambda nm, shp, dt=F32: sco.enter_context(nc.psum_tensor(nm, shp, dt))
        wtile = [sb(f"d_wt{i}", [128, DM]) for i in range(2)]
        wtile_b = [R.buf(f"d_wt{i}") for i in range(2)]
        s1 = sb("d_s1", [128, DM])
        s1_b = R.buf("d_s1")
        s2 = sb("d_s2", [128, DM])
        s2_b = R.buf("d_s2")
        q2 = [sb(f"d_q2{i}", [128, DM], BF16) for i in range(2)]
        q2_b = [R.buf(f"d_q2{i}") for i in range(2)]
        acc_abs = sb("d_accabs", [128, 2])
        acc_abs_b = [R.buf("d_aab0"), R.buf("d_aab1")]
        red = sb("d_red", [128, 1])
        red_b = R.buf("d_red")
        thr4 = sb("d_thr4", [1, 4])
        thr4_b = R.buf("d_thr4")
        amrow = sb("d_amrow", [1, 2])
        amrow_b = R.buf("d_amrow")
        srow = sb("d_srow", [1, 2])
        srow_b = R.buf("d_srow")
        tmp11 = sb("d_tmp11", [1, 1])
        tmp11_b = R.buf("d_tmp11")
        am_ps = ps("d_am_ps", [1, 2])
        am_ps_b = R.buf("d_am_ps")
        bc_ps = ps("d_bc_ps", [128, 4])
        bc_ps_b = R.buf("d_bc_ps")
        hsc = _mk_hilo(R, sb, "d_hsc", 128, 4)

        mats = ((dr["f1_wfc"], q2fc, q2fc_b, 0), (dr["f1_wpr"], q2pr, q2pr_b, 1))
        for src, dst, dst_b, col in mats:
            R.op("gpsimd", lambda e, _c=col: e.memset(acc_abs[:, _c:_c + 1], 0.0),
                 outs=[acc_abs_b[col]])
            for t in range(NHC):
                w, wb = wtile[t % 2], wtile_b[t % 2]
                R.dma("scalar", w[:, :], src[t, :, :], outs=[wb])
                R.op("vector", lambda e, _w=w: e.tensor_reduce(
                    red[:, 0:1], _w[:, :], AX.X, ALU.add, apply_absolute_value=True),
                    ins=[wb], outs=[red_b])
                R.op("vector", lambda e, _c=col: e.tensor_tensor(
                    acc_abs[:, _c:_c + 1], acc_abs[:, _c:_c + 1], red[:, 0:1], ALU.add),
                    ins=[red_b, acc_abs_b[col]], outs=[acc_abs_b[col]])
            _colsum(R, C, hsc, am_ps[0:1, col:col + 1], am_ps_b,
                    acc_abs[:, col:col + 1], [acc_abs_b[col]])
        nel = float(NHC * 128 * DM)
        R.op("vector", lambda e: e.tensor_scalar(amrow[0:1, 0:2], am_ps[0:1, 0:2],
                                                 1.0 / nel, 1e-5, ALU.mult, ALU.max),
             ins=[am_ps_b], outs=[amrow_b])
        for i, (c0, m0) in enumerate(((0, 0.7), (0, -0.7), (1, 0.7), (1, -0.7))):
            R.op("vector", lambda e, _i=i, _c=c0, _m=m0: e.tensor_scalar(
                thr4[0:1, _i:_i + 1], amrow[0:1, _c:_c + 1], _m, None, ALU.mult),
                ins=[amrow_b], outs=[thr4_b])
        R.op("vector", lambda e: e.tensor_scalar(srow[0:1, 0:2], amrow[0:1, 0:2],
                                                 0.5, None, ALU.mult),
             ins=[amrow_b], outs=[srow_b])
        _bcast_row(R, C, hsc, bc_ps[:, 0:4], bc_ps_b, thr4[0:1, 0:4], [thr4_b], 4)
        R.op("vector", lambda e: e.tensor_copy(thr_cols[:, :], bc_ps[:, 0:4]),
             ins=[bc_ps_b], outs=[thr_cols_b])
        _bcast_row(R, C, hsc, bc_ps[:, 0:2], bc_ps_b, srow[0:1, 0:2], [srow_b], 2)
        R.op("scalar", lambda e: e.activation(csc[:, :], bc_ps[:, 0:2], AF.Copy),
             ins=[bc_ps_b], outs=[csc_b])

        if DEBUG_TAPS:
            R.dma("sync", dbg_row[:, 0:4], thr_cols[:, :], ins=[thr_cols_b])
            R.dma("sync", dbg_row[:, 4:6], csc[:, :], ins=[csc_b])
        for src, dst, dst_b, col in mats:
            tc = 2 * col
            for t in range(NHC):
                w, wb = wtile[t % 2], wtile_b[t % 2]
                qq, qqb = q2[t % 2], q2_b[t % 2]
                R.dma("scalar", w[:, :], src[t, :, :], outs=[wb])
                R.op("scalar", lambda e, _w=w, _c=tc: e.activation(
                    s1[:, :], _w[:, :], AF.Sign, bias=thr_cols[:, _c + 1:_c + 2]),
                    ins=[wb, thr_cols_b], outs=[s1_b], risky=False)
                R.op("scalar", lambda e, _w=w, _c=tc: e.activation(
                    s2[:, :], _w[:, :], AF.Sign, bias=thr_cols[:, _c:_c + 1]),
                    ins=[wb, thr_cols_b], outs=[s2_b], risky=False)
                R.op("vector", lambda e, _q=qq: e.tensor_tensor(
                    _q[:, :], s1[:, :], s2[:, :], ALU.add),
                    ins=[s1_b, s2_b], outs=[qqb], risky=False)
                R.dma("scalar", dst[t, :, :], qq[:, :], ins=[qqb], outs_accum=[dst_b])

    return {"es": sco, "q2fc": q2fc, "q2pr": q2pr, "q2fc_b": q2fc_b,
            "q2pr_b": q2pr_b, "csc": csc, "csc_b": csc_b}


def emit_phaseD(R, C, dr, xtok_dram, out_dram, last_phase=True, tern=None):
    nc = C.nc
    if tern is None:
        tern = emit_phaseD_tern(R, C, dr)
        tern["es"].close()
        R.barrier()
    q2fc, q2pr = tern["q2fc"], tern["q2pr"]
    q2fc_b, q2pr_b = tern["q2fc_b"], tern["q2pr_b"]
    csc, csc_b = tern["csc"], tern["csc_b"]

    with ExitStack() as sco:
        sb = lambda nm, shp, dt=F32: sco.enter_context(nc.sbuf_tensor(nm, shp, dt))
        ps = lambda nm, shp, dt=F32: sco.enter_context(nc.psum_tensor(nm, shp, dt))
        hT = sb("d_hT", [128, NHC * 512], BF16)
        hT_b = [R.buf(f"d_hT{i}") for i in range(NHC)]
        qw = [sb(f"d_qw{i}", [128, DM], BF16) for i in range(2)]
        qw_b = [R.buf(f"d_qw{i}") for i in range(2)]
        qp = [sb(f"d_qp{i}", [128, 4 * 1024], BF16) for i in range(2)]
        qp_b = [R.buf(f"d_qp{i}") for i in range(2)]
        relu_s = [sb(f"d_relu{i}", [128, 512]) for i in range(2)]
        relu_b = [R.buf(f"d_relu{i}") for i in range(2)]
        yq = sb("d_yq", [128, 4 * DM])
        yq_b = [R.buf(f"d_yq{i}") for i in range(4)]
        ybf = sb("d_ybf", [128, DM], BF16)
        ybf_b = R.buf("d_ybf")
        dps = [ps(f"d_dps{i}", [128, 512]) for i in range(8)]
        dps_b = [R.buf(f"d_dps{i}") for i in range(8)]
        lnsc = _mk_ln_scratch(R, "d_ln", alloc=sb)

        for sq in range(4):
            if DEBUG_TAPS and sq == 1:
                R.dma("sync", dbg_hT[:, :], hT[:, :], ins=hT_b)
                for dc in range(NDC):
                    R.dma("sync", dbg_xT[:, dc * S:(dc + 1) * S], C.xTa(dc),
                          ins=[C.xT_b[dc]])
            for sti in range(4):
                st = sq * 4 + sti
                R.dma("sync", yq[:, sti * DM:(sti + 1) * DM],
                      xtok_dram[st * 128:(st + 1) * 128, :], outs=[yq_b[sti]])
            for hc in range(NHC):
                w, wb = qw[hc % 2], qw_b[hc % 2]
                R.dma("gpsimd", w[:, :], q2fc[hc, :, :], ins=[q2fc_b], outs=[wb])
                pp, ppb = dps[hc % 2], dps_b[hc % 2]
                R.mm(pp[:, :],
                     [(w[:, dc * 128:(dc + 1) * 128],
                       C.xTa(dc)[:, sq * 512:(sq + 1) * 512]) for dc in range(NDC)],
                     ins=[wb] + C.xT_b, outs=[ppb])
                rl, rlb = relu_s[hc % 2], relu_b[hc % 2]
                R.op("scalar", lambda e, _p=pp, _r=rl: e.activation(
                    _r[:, :], _p[:, :], AF.Relu), ins=[ppb, csc_b], outs=[rlb],
                    risky=False)
                R.op("scalar", lambda e, _hc=hc, _r=rl: e.activation(
                    hT[:, _hc * 512:(_hc + 1) * 512], _r[:, :], AF.Square,
                    scale=csc[:, 0:1]),
                    ins=[rlb, csc_b], outs=[hT_b[hc]], risky=False)
            for dh in range(2):
                for hb in range(0, NHC, 4):
                    g_i = (hb // 4) % 2
                    w2, w2b = qp[g_i], qp_b[g_i]
                    R.dma("gpsimd", w2[:, :].rearrange("p (h d) -> p h d", h=4),
                          q2pr[hb:hb + 4, :, dh * 1024:(dh + 1) * 1024]
                          .rearrange("h p d -> p h d"),
                          ins=[q2pr_b], outs=[w2b])
                    for icl in range(4):
                        hc = hb + icl
                        for sti in range(4):
                            for dcl in range(2):
                                k = sti * 2 + dcl
                                R.mm(dps[k][:, :],
                                     [(hT[:, hc * 512 + sti * 128:
                                           hc * 512 + (sti + 1) * 128],
                                       w2[:, icl * 1024 + dcl * 512:
                                          icl * 1024 + (dcl + 1) * 512])],
                                     ins=[w2b, hT_b[hc]], outs=[dps_b[k]],
                                     first=(hc == 0), last=(hc == NHC - 1))
                for sti in range(4):
                    for dcl in range(2):
                        k = sti * 2 + dcl
                        c0 = sti * DM + dh * 1024 + dcl * 512
                        R.op("vector", lambda e, _k=k, _c=c0: e.scalar_tensor_tensor(
                            yq[:, _c:_c + 512], dps[_k][:, :], csc[:, 1:2],
                            yq[:, _c:_c + 512], ALU.mult, ALU.add),
                            ins=[dps_b[k], csc_b, yq_b[sti]], outs=[yq_b[sti]],
                            risky=False)
            for sti in range(4):
                st = sq * 4 + sti
                yap = yq[:, sti * DM:(sti + 1) * DM]
                _ln_tile(R, C, lnsc, yap, yq_b[sti])
                _store_tile(R, C, st, yap, yq_b[sti], out_dram, last_phase, ybf, ybf_b)





def emit_phaseB(R, C, dr, xtok_dram, out_dram, last_phase=False):
    nc = C.nc
    with ExitStack() as sco:
        sb = lambda nm, shp, dt=F32: sco.enter_context(nc.sbuf_tensor(nm, shp, dt))
        comb = sb("b_comb", [128, NST * NE])
        comb_b = [R.buf(f"b_comb{st}") for st in range(NST)]
        sg = sb("b_sg", [128, NST])
        sg_b = [R.buf(f"b_sg{st}") for st in range(NST)]

        with ExitStack() as rsco:
            rsb = lambda nm, shp, dt=F32: rsco.enter_context(nc.sbuf_tensor(nm, shp, dt))
            rps = lambda nm, shp, dt=F32: rsco.enter_context(nc.psum_tensor(nm, shp, dt))
            gwT = rsb("b_gwT", [128, NDC * NE], BF16)
            gwT_b = R.buf("b_gwT")
            sgw = rsb("b_sgw", [128, NDC], BF16)
            sgw_b = R.buf("b_sgw")
            R.dma("gpsimd", gwT[:, :], dr["f0_gwT"][:, :], outs=[gwT_b])
            R.dma("gpsimd", sgw[:, :], dr["f0_sgw"][:, :], outs=[sgw_b])
            lp = rps("b_lp", [128, NE])
            lp_b = R.buf("b_lp")
            sgp = rps("b_sgp", [128, 1])
            sgp_b = R.buf("b_sgp")
            pr = rsb("b_pr", [128, NE])
            pr_b = R.buf("b_pr")
            rc1 = rsb("b_rc1", [128, 6])
            rc1_b = R.buf("b_rc1")
            oh = rsb("b_oh", [128, 2 * NE])
            oh_b = R.buf("b_oh")
            for st in range(NST):
                xs = lambda dc, _st=st: C.xTa(dc)[:, _st * 128:(_st + 1) * 128]
                R.mm(lp[:, :], [(xs(dc), gwT[:, dc * NE:(dc + 1) * NE])
                                for dc in range(NDC)],
                     ins=[gwT_b] + C.xT_b, outs=[lp_b])
                R.op("vector", lambda e: e.tensor_reduce(rc1[:, 0:1], lp[:, :], AX.X,
                                                         ALU.max, negate=True),
                     ins=[lp_b], outs=[rc1_b])
                R.op("scalar", lambda e: e.activation(pr[:, :], lp[:, :], AF.Exp,
                                                      bias=rc1[:, 0:1],
                                                      accum_out=rc1[:, 1:2]),
                     ins=[lp_b, rc1_b], outs=[pr_b, rc1_b])
                R.op("vector", lambda e: e.reciprocal(rc1[:, 2:3], rc1[:, 1:2]),
                     ins=[rc1_b], outs=[rc1_b])
                R.op("vector", lambda e: e.tensor_scalar(pr[:, :], pr[:, :],
                                                         rc1[:, 2:3], None, ALU.mult),
                     ins=[pr_b, rc1_b], outs=[pr_b])
                R.op("vector", lambda e: e.tensor_reduce(rc1[:, 3:4], pr[:, :],
                                                         AX.X, ALU.max),
                     ins=[pr_b], outs=[rc1_b])
                R.op("vector", lambda e: e.tensor_scalar(oh[:, 0:NE], pr[:, :],
                                                         rc1[:, 3:4], None, ALU.is_ge),
                     ins=[pr_b, rc1_b], outs=[oh_b])
                R.op("vector", lambda e: e.tensor_scalar(oh[:, NE:2 * NE], oh[:, 0:NE],
                                                         -1.0, 1.0, ALU.mult, ALU.add),
                     ins=[oh_b], outs=[oh_b])
                R.op("vector", lambda e: e.tensor_tensor(oh[:, NE:2 * NE], pr[:, :],
                                                         oh[:, NE:2 * NE], ALU.mult),
                     ins=[pr_b, oh_b], outs=[oh_b])
                R.op("vector", lambda e: e.tensor_reduce(rc1[:, 4:5], oh[:, NE:2 * NE],
                                                         AX.X, ALU.max),
                     ins=[oh_b], outs=[rc1_b])
                R.op("vector", lambda e: e.tensor_tensor(rc1[:, 5:6], rc1[:, 3:4],
                                                         rc1[:, 4:5], ALU.add),
                     ins=[rc1_b], outs=[rc1_b])
                R.op("vector", lambda e: e.reciprocal(rc1[:, 5:6], rc1[:, 5:6]),
                     ins=[rc1_b], outs=[rc1_b])
                R.op("vector", lambda e: e.tensor_scalar(oh[:, NE:2 * NE],
                                                         oh[:, NE:2 * NE],
                                                         rc1[:, 4:5], None, ALU.is_ge),
                     ins=[oh_b, rc1_b], outs=[oh_b])
                R.op("vector", lambda e, _st=st: e.tensor_scalar(
                    comb[:, _st * NE:(_st + 1) * NE], oh[:, 0:NE], rc1[:, 3:4],
                    None, ALU.mult),
                    ins=[oh_b, rc1_b], outs=[comb_b[st]])
                R.op("vector", lambda e: e.tensor_scalar(oh[:, NE:2 * NE],
                                                         oh[:, NE:2 * NE],
                                                         rc1[:, 4:5], None, ALU.mult),
                     ins=[oh_b, rc1_b], outs=[oh_b])
                R.op("vector", lambda e, _st=st: e.tensor_tensor(
                    comb[:, _st * NE:(_st + 1) * NE],
                    comb[:, _st * NE:(_st + 1) * NE], oh[:, NE:2 * NE], ALU.add),
                    ins=[oh_b, comb_b[st]], outs=[comb_b[st]])
                R.op("vector", lambda e, _st=st: e.tensor_scalar(
                    comb[:, _st * NE:(_st + 1) * NE],
                    comb[:, _st * NE:(_st + 1) * NE], rc1[:, 5:6], None, ALU.mult),
                    ins=[comb_b[st], rc1_b], outs=[comb_b[st]])
                R.mm(sgp[:, 0:1], [(xs(dc), sgw[:, dc:dc + 1]) for dc in range(NDC)],
                     ins=[sgw_b] + C.xT_b, outs=[sgp_b])
                R.op("scalar", lambda e, _st=st: e.activation(
                    sg[:, _st:_st + 1], sgp[:, 0:1], AF.Sigmoid),
                    ins=[sgp_b], outs=[sg_b[st]])

        R.barrier()

        ps = lambda nm, shp, dt=F32: sco.enter_context(nc.psum_tensor(nm, shp, dt))
        acc = sb("b_acc", [128, 8 * DM])
        acc_b = [R.buf(f"b_acc{i}") for i in range(8)]
        hbuf = sb("b_hbuf", [128, NSC * 512], BF16)
        hbuf_b = [R.buf(f"b_h{i}") for i in range(NSC)]
        wpair = [sb(f"b_w{i}", [128, NDC * 128], BF16) for i in range(4)]
        wpair_b = [R.buf(f"b_w{i}") for i in range(4)]
        wdb = [sb(f"b_wd{i}", [128, 4 * 512], BF16) for i in range(2)]
        wdb_b = [R.buf(f"b_wd{i}") for i in range(2)]
        silu_t = [sb(f"b_silu{i}", [128, 512]) for i in range(2)]
        silu_b = [R.buf(f"b_silu{i}") for i in range(2)]
        ps8 = [ps(f"b_ps{i}", [128, 512]) for i in range(8)]
        ps8_b = [R.buf(f"b_ps{i}") for i in range(8)]
        ybf = sb("b_ybf", [128, DM], BF16)
        ybf_b = R.buf("b_ybf")
        lnsc = _mk_ln_scratch(R, "b_ln", alloc=sb, sq=(ybf, ybf_b))

        def gu_block(wbase, wg_src, wu_src, jobs, icpar=0):
            wg_t, wgb = wpair[wbase], wpair_b[wbase]
            wu_t, wub = wpair[wbase + 1], wpair_b[wbase + 1]
            R.dma("gpsimd", wg_t[:, :], wg_src, outs=[wgb])
            R.dma("gpsimd", wu_t[:, :], wu_src, outs=[wub])
            for i, (c0, x0) in enumerate(jobs):
                bk = 4 * icpar + 2 * (i % 2)
                gp, gpb = ps8[bk], ps8_b[bk]
                up, upb = ps8[bk + 1], ps8_b[bk + 1]
                sl, slb = silu_t[i % 2], silu_b[i % 2]
                R.mm(gp[:, :], [(wg_t[:, dc * 128:(dc + 1) * 128],
                                 C.xTa(dc)[:, x0:x0 + 512]) for dc in range(NDC)],
                     ins=[wgb] + C.xT_b, outs=[gpb])
                R.mm(up[:, :], [(wu_t[:, dc * 128:(dc + 1) * 128],
                                 C.xTa(dc)[:, x0:x0 + 512]) for dc in range(NDC)],
                     ins=[wub] + C.xT_b, outs=[upb])
                R.op("scalar", lambda e, _g=gp, _s=sl: e.activation(
                    _s[:, :], _g[:, :], AF.Silu), ins=[gpb], outs=[slb], risky=False)
                R.op("vector", lambda e, _u=up, _s=sl, _c=c0: e.tensor_tensor(
                    hbuf[:, _c:_c + 512], _s[:, :], _u[:, :], ALU.mult),
                    ins=[slb, upb], outs=[hbuf_b[c0 // 512]], risky=False)

        def down_block(nic_, wd_src_fn, hoff_fn, jobs):
            for dq in range(4):
                for ic0 in range(0, nic_, 4):
                    nb = min(4, nic_ - ic0)
                    g_i = (ic0 // 4) % 2
                    wd_t, wdbb = wdb[g_i], wdb_b[g_i]
                    R.dma("gpsimd",
                          wd_t[:, 0:nb * 512].rearrange("p (i d) -> p i d", i=nb),
                          wd_src_fn(ic0, nb, dq).rearrange("i p d -> p i d"),
                          outs=[wdbb])
                    for icl in range(nb):
                        ic = ic0 + icl
                        for si, (hoff, ai, g_ap, g_bufs) in enumerate(jobs):
                            col = hoff_fn(ic) + hoff
                            R.mm(ps8[si][:, :],
                                 [(hbuf[:, col:col + 128],
                                   wd_t[:, icl * 512:(icl + 1) * 512])],
                                 ins=[wdbb, hbuf_b[col // 512]], outs=[ps8_b[si]],
                                 first=(ic == 0), last=(ic == nic_ - 1))
                for si, (hoff, ai, g_ap, g_bufs) in enumerate(jobs):
                    R.op("vector", lambda e, _si=si, _ai=ai, _g=g_ap, _dq=dq:
                         e.scalar_tensor_tensor(
                             acc[:, _ai * DM + _dq * 512: _ai * DM + _dq * 512 + 512],
                             ps8[_si][:, :], _g,
                             acc[:, _ai * DM + _dq * 512: _ai * DM + _dq * 512 + 512],
                             ALU.mult, ALU.add),
                         ins=[ps8_b[si], acc_b[ai]] + list(g_bufs),
                         outs=[acc_b[ai]], risky=False)

        for hf in range(2):
            for stl in range(8):
                st = hf * 8 + stl
                R.dma("sync", acc[:, stl * DM:(stl + 1) * DM],
                      xtok_dram[st * 128:(st + 1) * 128, :], outs=[acc_b[stl]])
            for sq2 in range(2):
                sq = hf * 2 + sq2
                for ic in range(NSC):
                    gu_block(2 * (ic % 2), dr["f0_swg"][ic, :, :],
                             dr["f0_swu"][ic, :, :], [(ic * 512, sq * 512)],
                             icpar=(ic // 2) % 2)
                jobs = []
                for i in range(4):
                    st = sq * 4 + i
                    jobs.append((i * 128, sq2 * 4 + i, sg[:, st:st + 1], [sg_b[st]]))
                down_block(NSC, lambda ic0, nb, dq: dr["f0_swd"][dq, ic0:ic0 + nb, :, :],
                           lambda ic: ic * 512, jobs)
            for e_i in range(NE):
                for ic in range(NIC):
                    gu_block(2 * (ic % 2), dr["f0_wg"][e_i, ic, :, :],
                             dr["f0_wu"][e_i, ic, :, :],
                             [(ic * 1024, hf * 1024), (ic * 1024 + 512, hf * 1024 + 512)],
                             icpar=ic % 2)
                jobs = []
                for stl in range(8):
                    st = hf * 8 + stl
                    jobs.append((stl * 128, stl,
                                 comb[:, st * NE + e_i: st * NE + e_i + 1],
                                 [comb_b[st]]))
                down_block(NIC,
                           lambda ic0, nb, dq, _e=e_i: dr["f0_wd"][_e, dq,
                                                                   ic0:ic0 + nb, :, :],
                           lambda ic: ic * 1024, jobs)
            for stl in range(8):
                st = hf * 8 + stl
                yap = acc[:, stl * DM:(stl + 1) * DM]
                _ln_tile(R, C, lnsc, yap, acc_b[stl])
                _store_tile(R, C, st, yap, acc_b[stl], out_dram, last_phase, ybf, ybf_b)





def emit_phaseA(R, C, dr, xtok_dram, out_dram, last_phase=False):
    nc = C.nc
    SCALE = float(HD) ** -0.5
    att_hbm = nc.dram_tensor("a0_att_scr", [NST, 128, H * 128], BF16, kind="Internal")
    att_hbm_b = [R.buf(f"a0_att{h}") for h in range(H)]

    with ExitStack() as sco:
        sb = lambda nm, shp, dt=F32: sco.enter_context(nc.sbuf_tensor(nm, shp, dt))
        ps = lambda nm, shp, dt=F32: sco.enter_context(nc.psum_tensor(nm, shp, dt))
        trilA = sb("a_trilA", [128, 128])
        trilA_b = R.buf("a_trilA")
        R.dma("gpsimd", trilA[:, :], dr["c_trilA"][:, :], outs=[trilA_b])
        grow = sb("a_grow", [1, 2 * H])
        grow_b = R.buf("a_grow")
        R.dma("gpsimd", grow[0:1, 0:H], dr["a0_gate"][0:1, :], outs=[grow_b])
        R.op("scalar", lambda e: e.activation(grow[0:1, 0:H], grow[0:1, 0:H],
                                              AF.Sigmoid), ins=[grow_b], outs=[grow_b])
        R.op("vector", lambda e: e.tensor_scalar(grow[0:1, H:2 * H], grow[0:1, 0:H],
                                                 -1.0, 1.0, ALU.mult, ALU.add),
             ins=[grow_b], outs=[grow_b])
        normsf = sb("a_normsf", [NM * H, 128])
        normsf_b = R.buf("a_normsf")
        mselb = sb("a_msel", [NM * H, NM], BF16)
        mselb_b = R.buf("a_msel")
        R.dma("gpsimd", normsf[:, :], dr["a0_normsf"][:, :], outs=[normsf_b])
        R.dma("gpsimd", mselb[:, :], dr["c_msel"][:, :], outs=[mselb_b])
        rsum_f = sb("a_rsumf", [NM * H, 1])
        rsum_fb = R.buf("a_rsumf")
        rsum = sb("a_rsum", [NM * H, 1], BF16)
        rsum_b = R.buf("a_rsum")
        R.op("vector", lambda e: e.tensor_reduce(rsum_f[:, 0:1], normsf[:, :], AX.X,
                                                 ALU.add),
             ins=[normsf_b], outs=[rsum_fb])
        R.op("vector", lambda e: e.tensor_copy(rsum[:, 0:1], rsum_f[:, 0:1]),
             ins=[rsum_fb], outs=[rsum_b])
        mrow = sb("a_mrow", [1, NM])
        mrow_b = R.buf("a_mrow")
        m1e30 = sb("a_m1e30", [1, NM])
        m1e30_b = R.buf("a_m1e30")
        negeps = sb("a_negeps", [1, 1])
        negeps_b = R.buf("a_negeps")
        R.op("gpsimd", lambda e: e.memset(negeps[0:1, 0:1], -1e-6), outs=[negeps_b])
        mps_es = ExitStack()
        mps = mps_es.enter_context(nc.psum_tensor("a_mps", [1, NM], F32))
        mps_b = R.buf("a_mps")
        R.mm(mps[0:1, :], [(rsum[:, 0:1], mselb[:, :])],
             ins=[rsum_b, mselb_b], outs=[mps_b])
        R.op("scalar", lambda e: e.activation(mrow[0:1, :], mps[0:1, :], AF.Sign,
                                              bias=negeps[0:1, 0:1]),
             ins=[mps_b, negeps_b], outs=[mrow_b])
        R.op("vector", lambda e: e.tensor_scalar(mrow[0:1, :], mrow[0:1, :], 0.0,
                                                 None, ALU.max),
             ins=[mrow_b], outs=[mrow_b])
        R.op("vector", lambda e: e.tensor_scalar(m1e30[0:1, :], mrow[0:1, :], 1e30,
                                                 -1e30, ALU.mult, ALU.add),
             ins=[mrow_b], outs=[m1e30_b])
        xbar_f = sb("a_xbarf", [128, NDC])
        xbar_fb = R.buf("a_xbarf")
        xbar = sb("a_xbar", [128, NDC], BF16)
        xbar_b = R.buf("a_xbar")
        for dc in range(NDC):
            R.op("vector", lambda e, _dc=dc: e.tensor_reduce(
                xbar_f[:, _dc:_dc + 1], C.xTa(_dc), AX.X, ALU.add),
                ins=[C.xT_b[dc]], outs=[xbar_fb])
        R.op("vector", lambda e: e.tensor_scalar(xbar[:, :], xbar_f[:, :], 1.0 / S,
                                                 None, ALU.mult),
             ins=[xbar_fb], outs=[xbar_b])
        mps_es.close()
        R.barrier()

        wq = [sb(f"a_wq{i}", [128, NDC * HD], BF16) for i in range(2)]
        wq_b = [R.buf(f"a_wq{i}") for i in range(2)]
        wk = [sb(f"a_wk{i}", [128, NDC * HD], BF16) for i in range(2)]
        wk_b = [R.buf(f"a_wk{i}") for i in range(2)]
        wv = [sb(f"a_wv{i}", [128, NDC * HD], BF16) for i in range(2)]
        wv_b = [R.buf(f"a_wv{i}") for i in range(2)]
        memb = [sb(f"a_mem{i}", [128, NM * HD], BF16) for i in range(2)]
        memb_b = [R.buf(f"a_mem{i}") for i in range(2)]
        nrmb = [sb(f"a_nrm{i}", [128, NM], BF16) for i in range(2)]
        nrmb_b = [R.buf(f"a_nrm{i}") for i in range(2)]
        sqT = sb("a_sqT", [128, S], BF16)
        sqT_b = R.buf("a_sqT")
        skT = sb("a_skT", [128, S], BF16)
        skT_b = R.buf("a_skT")
        kp_tok = sb("a_kptok", [128, S], BF16)
        kp_tok_b = R.buf("a_kptok")
        v_tok = sb("a_vtok", [128, S], BF16)
        v_tok_b = R.buf("a_vtok")
        att_acc = sb("a_attacc", [128, S])
        att_acc_bs = [R.buf(f"a_attacc{i}") for i in range(NST)]
        att_bf = sb("a_attbf", [128, S], BF16)
        att_bf_bs = [R.buf(f"a_attbf{i}") for i in range(NST)]
        attnT = sb("a_attnT", [128, S], BF16)
        attnT_b = R.buf("a_attnT")
        S_f = sb("a_Sf", [128, HD])
        S_f_b = R.buf("a_Sf")
        S_bf = sb("a_Sbf", [128, HD], BF16)
        S_bf_b = R.buf("a_Sbf")
        Z_f = sb("a_Zf", [128, 1])
        Z_f_b = R.buf("a_Zf")
        Z_bf = sb("a_Zbf", [128, 1], BF16)
        Z_bf_b = R.buf("a_Zbf")
        scr = sb("a_scr", [128, 512])
        scr_b = R.buf("a_scr")
        eq = sb("a_eq", [128, 512])
        eq_b = R.buf("a_eq")
        rq = sb("a_rq", [128, 512])
        rq_b = R.buf("a_rq")
        qbarc = sb("a_qbarc", [128, 1], BF16)
        qbarc_b = R.buf("a_qbarc")
        relrow = sb("a_relrow", [1, 8])
        relrow_b = R.buf("a_relrow")
        wrow = sb("a_wrow", [1, 5], BF16)
        wrow_b = R.buf("a_wrow")
        wrow_f = sb("a_wrowf", [1, 5])
        wrow_fb = R.buf("a_wrowf")
        wgc = sb("a_wgc", [128, 5])
        wgc_b = R.buf("a_wgc")
        den_t = sb("a_dent", [128, 5])
        den_t_b = R.buf("a_dent")
        nr4 = sb("a_nr4", [128, 8])
        nr4_b = R.buf("a_nr4")
        hsc = _mk_hilo(R, sb, "a_hsc", 1, 5)
        pA = ps("a_pA", [128, 512])
        pA_b = R.buf("a_pA")
        pB = ps("a_pB", [128, 512])
        pB_b = R.buf("a_pB")
        pC = ps("a_pC", [128, 512])
        pC_b = R.buf("a_pC")
        pD = ps("a_pD", [128, 512])
        pD_b = R.buf("a_pD")
        pS = ps("a_pS", [128, 8])
        pS_b = R.buf("a_pS")
        pE = ps("a_pE", [128, 128])
        pE_b = R.buf("a_pE")

        for h in range(H):
            p = h % 2
            R.dma("gpsimd", wq[p][:, :], dr["a0_q"][h, :, :], outs=[wq_b[p]])
            R.dma("gpsimd", wk[p][:, :], dr["a0_k"][h, :, :], outs=[wk_b[p]])
            R.dma("gpsimd", wv[p][:, :], dr["a0_v"][h, :, :], outs=[wv_b[p]])
            R.dma("gpsimd", memb[p][:, :], dr["a0_mem"][h, :, :], outs=[memb_b[p]])
            R.dma("gpsimd", nrmb[p][:, :], dr["a0_norms"][h, :, :], outs=[nrmb_b[p]])

            for n in range(NST):
                R.mm(pC[:, 0:128],
                     [(C.xTa(dc)[:, n * 128:(n + 1) * 128],
                       wv[p][:, dc * HD:(dc + 1) * HD]) for dc in range(NDC)],
                     ins=[wv_b[p]] + C.xT_b, outs=[pC_b])
                R.op("scalar", lambda e, _n=n: e.activation(
                    v_tok[:, _n * 128:(_n + 1) * 128], pC[:, 0:128], AF.Copy),
                    ins=[pC_b], outs=[v_tok_b], risky=False)
            for sc_i in range(4):
                c0 = sc_i * 512
                for (w_t, wbuf, pp, ppb, dstT, dstT_b, is_v) in (
                        (wq[p], wq_b[p], pA, pA_b, sqT, sqT_b, False),
                        (wk[p], wk_b[p], pB, pB_b, skT, skT_b, False)):
                    R.mm(pp[:, :], [(w_t[:, dc * HD:(dc + 1) * HD],
                                     C.xTa(dc)[:, c0:c0 + 512]) for dc in range(NDC)],
                         ins=[wbuf] + C.xT_b, outs=[ppb])
                    if is_v:
                        pass
                    else:
                        R.op("vector", lambda e, _p=pp: e.tensor_scalar(
                            scr[:, :], _p[:, :], 0.0, None, ALU.min),
                            ins=[ppb], outs=[scr_b], risky=False)
                        R.op("scalar", lambda e: e.activation(eq[:, :], scr[:, :],
                                                              AF.Exp),
                             ins=[scr_b], outs=[eq_b], risky=False)
                        R.op("vector", lambda e, _p=pp: e.tensor_relu(
                            rq[:, :], _p[:, :]),
                            ins=[ppb], outs=[rq_b], risky=False)
                        R.op("vector", lambda e, _d=dstT, _c=c0: e.tensor_tensor(
                            _d[:, _c:_c + 512], eq[:, :], rq[:, :], ALU.add),
                            ins=[eq_b, rq_b], outs=[dstT_b], risky=False)
            for n in range(NST):
                R.dma("scalar", kp_tok[:, n * 128:(n + 1) * 128],
                      skT[:, n * 128:(n + 1) * 128], ins=[skT_b],
                      outs_accum=[kp_tok_b], transpose=True)
            R.mm(pS[:, 0:1], [(wq[p][:, dc * HD:(dc + 1) * HD], xbar[:, dc:dc + 1])
                              for dc in range(NDC)],
                 ins=[wq_b[p], xbar_b], outs=[pS_b])
            R.op("vector", lambda e: e.tensor_scalar(qbarc[:, 0:1], pS[:, 0:1],
                                                     SCALE, None, ALU.mult),
                 ins=[pS_b], outs=[qbarc_b])
            R.mm(pS[0:1, 4:8], [(qbarc[:, 0:1], nrmb[p][:, :])],
                 ins=[qbarc_b, nrmb_b[p]], outs=[pS_b])
            R.op("vector", lambda e: e.tensor_tensor(relrow[0:1, 0:4], pS[0:1, 4:8],
                                                     mrow[0:1, :], ALU.mult),
                 ins=[pS_b, mrow_b], outs=[relrow_b])
            R.op("vector", lambda e: e.tensor_tensor(relrow[0:1, 0:4],
                                                     relrow[0:1, 0:4],
                                                     m1e30[0:1, :], ALU.add),
                 ins=[relrow_b, m1e30_b], outs=[relrow_b])
            R.op("vector", lambda e: e.tensor_reduce(relrow[0:1, 4:5],
                                                     relrow[0:1, 0:4], AX.X,
                                                     ALU.max, negate=True),
                 ins=[relrow_b], outs=[relrow_b])
            R.op("scalar", lambda e: e.activation(relrow[0:1, 0:4], relrow[0:1, 0:4],
                                                  AF.Exp, bias=relrow[0:1, 4:5],
                                                  accum_out=relrow[0:1, 5:6]),
                 ins=[relrow_b], outs=[relrow_b])
            R.op("vector", lambda e: e.reciprocal(relrow[0:1, 6:7], relrow[0:1, 5:6]),
                 ins=[relrow_b], outs=[relrow_b])
            R.op("vector", lambda e: e.tensor_scalar(relrow[0:1, 0:4],
                                                     relrow[0:1, 0:4],
                                                     relrow[0:1, 6:7], None, ALU.mult),
                 ins=[relrow_b], outs=[relrow_b])
            R.op("vector", lambda e: e.tensor_tensor(relrow[0:1, 0:4],
                                                     relrow[0:1, 0:4], mrow[0:1, :],
                                                     ALU.mult),
                 ins=[relrow_b, mrow_b], outs=[relrow_b])
            R.op("vector", lambda e, _h=h: e.tensor_scalar(
                wrow_f[0:1, 0:4], relrow[0:1, 0:4], grow[0:1, _h:_h + 1], None,
                ALU.mult),
                ins=[relrow_b, grow_b], outs=[wrow_fb])
            R.op("vector", lambda e, _h=h: e.tensor_copy(
                wrow_f[0:1, 4:5], grow[0:1, H + _h:H + _h + 1]),
                ins=[grow_b], outs=[wrow_fb])
            _bcast_row(R, C, hsc, pS[:, 0:5], pS_b, wrow_f[0:1, 0:5], [wrow_fb], 5)
            R.op("vector", lambda e: e.tensor_copy(wgc[:, :], pS[:, 0:5]),
                 ins=[pS_b], outs=[wgc_b])
            R.op("gpsimd", lambda e: e.memset(S_f[:, :], 0.0), outs=[S_f_b])
            R.op("gpsimd", lambda e: e.memset(S_bf[:, :], 0.0), outs=[S_bf_b])
            R.op("gpsimd", lambda e: e.memset(Z_f[:, :], 0.0), outs=[Z_f_b])
            R.op("gpsimd", lambda e: e.memset(Z_bf[:, :], 0.0), outs=[Z_bf_b])

            for n in range(NST):
                blk = slice(n * 128, (n + 1) * 128)
                sq_blk = sqT[:, blk]
                pSC, pSCb = (pA, pA_b) if n % 2 == 0 else (pE, pE_b)
                R.mm(pSC[:, 0:128], [(skT[:, blk], sq_blk)],
                     ins=[skT_b, sqT_b], outs=[pSCb])
                R.op("vector", lambda e, _b=blk, _p=pSC: e.tensor_tensor(
                    att_bf[:, _b], _p[:, 0:128], trilA[:, :], ALU.mult),
                    ins=[pSCb, trilA_b], outs=[att_bf_bs[n]], risky=False)
                scT = att_bf[:, blk]
                R.mm(pC[:, 0:128], [(kp_tok[:, blk], v_tok[:, blk])],
                     ins=[kp_tok_b, v_tok_b], outs=[pC_b])
                R.mm(pD[:, :], [(sq_blk, memb[p][:, :])],
                     ins=[sqT_b, memb_b[p]], outs=[pD_b])
                R.mm(pS[:, 4:8], [(sq_blk, nrmb[p][:, :])],
                     ins=[sqT_b, nrmb_b[p]], outs=[pS_b])
                R.mm(pS[:, 0:1], [(sq_blk, Z_bf[:, 0:1]), (scT, C.ones_col_b[:, 0:1])],
                     ins=[sqT_b, Z_bf_b, att_bf_bs[n]] + C.const_b, outs=[pS_b])
                R.mm(pB[:, 0:128], [(sq_blk, S_bf[:, :]), (scT, v_tok[:, blk])],
                     ins=[sqT_b, S_bf_b, att_bf_bs[n], v_tok_b], outs=[pB_b])
                R.op("vector", lambda e, _b=blk: e.tensor_reduce(den_t[:, 2:3],
                                                                 skT[:, _b],
                                                                 AX.X, ALU.add),
                     ins=[skT_b], outs=[den_t_b])
                R.op("vector", lambda e: e.tensor_scalar(den_t[:, 0:1], pS[:, 0:1],
                                                         1e-6, None, ALU.max),
                     ins=[pS_b], outs=[den_t_b])
                R.op("vector", lambda e: e.reciprocal(den_t[:, 1:2], den_t[:, 0:1]),
                     ins=[den_t_b], outs=[den_t_b])
                R.op("vector", lambda e: e.tensor_tensor(den_t[:, 1:2], den_t[:, 1:2],
                                                         wgc[:, 4:5], ALU.mult),
                     ins=[den_t_b, wgc_b], outs=[den_t_b])
                R.op("scalar", lambda e, _b=blk: e.activation(
                    att_acc[:, _b], pB[:, 0:128], AF.Copy, scale=den_t[:, 1:2]),
                    ins=[pB_b, den_t_b], outs=[att_acc_bs[n]], risky=False)
                R.op("vector", lambda e: e.tensor_scalar(nr4[:, 0:4], pS[:, 4:8],
                                                         1e-6, None, ALU.max),
                     ins=[pS_b], outs=[nr4_b])
                R.op("vector", lambda e: e.reciprocal(nr4[:, 4:8], nr4[:, 0:4]),
                     ins=[nr4_b], outs=[nr4_b])
                R.op("vector", lambda e: e.tensor_tensor(nr4[:, 4:8], nr4[:, 4:8],
                                                         wgc[:, 0:4], ALU.mult),
                     ins=[nr4_b, wgc_b], outs=[nr4_b])
                for m in range(NM):
                    R.op("vector", lambda e, _m=m, _b=blk: e.scalar_tensor_tensor(
                        att_acc[:, _b], pD[:, _m * 128:(_m + 1) * 128],
                        nr4[:, 4 + _m:5 + _m], att_acc[:, _b], ALU.mult, ALU.add),
                        ins=[pD_b, nr4_b, att_acc_bs[n]], outs=[att_acc_bs[n]],
                        risky=False)
                R.op("vector", lambda e: e.tensor_tensor(S_f[:, :], S_f[:, :],
                                                         pC[:, 0:128], ALU.add),
                     ins=[S_f_b, pC_b], outs=[S_f_b], risky=False)
                R.op("scalar", lambda e: e.activation(S_bf[:, :], S_f[:, :],
                                                      AF.Copy),
                     ins=[S_f_b], outs=[S_bf_b], risky=False)
                R.op("vector", lambda e: e.tensor_tensor(Z_f[:, 0:1], Z_f[:, 0:1],
                                                         den_t[:, 2:3], ALU.add),
                     ins=[Z_f_b, den_t_b], outs=[Z_f_b])
                R.op("vector", lambda e: e.tensor_copy(Z_bf[:, 0:1], Z_f[:, 0:1]),
                     ins=[Z_f_b], outs=[Z_bf_b])
                R.op("vector", lambda e, _b=blk: e.tensor_copy(att_bf[:, _b],
                                                               att_acc[:, _b]),
                     ins=[att_acc_bs[n]], outs=[att_bf_bs[n]], risky=False)
                R.dma("scalar", attnT[:, blk], att_bf[:, blk],
                      ins=[att_bf_bs[n]], outs_accum=[attnT_b], transpose=True)
            for g4 in range(4):
                R.dma("sync",
                      att_hbm[g4 * 4:(g4 + 1) * 4, :, h * 128:(h + 1) * 128]
                      .rearrange("s p e -> p s e"),
                      attnT[:, g4 * 512:(g4 + 1) * 512]
                      .rearrange("p (s e) -> p s e", s=4),
                      ins=[attnT_b], outs_accum=[att_hbm_b[h]])

    R.barrier()
    _oproj_epilogue(R, C, "a", att_hbm, att_hbm_b, dr["a0_wo"], xtok_dram,
                    out_dram, last_phase)




def _oproj_epilogue(R, C, tag, zh_hbm, zh_hbm_b, wo_dram, xtok_dram, out_dram,
                    last_phase):
    nc = C.nc
    with ExitStack() as sco:
        sb = lambda nm, shp, dt=F32: sco.enter_context(nc.sbuf_tensor(nm, shp, dt))
        ps = lambda nm, shp, dt=F32: sco.enter_context(nc.psum_tensor(nm, shp, dt))
        wo = sb(tag + "_wo", [128, H * DM], BF16)
        wo_b = R.buf(tag + "_wo")
        for h in range(H):
            R.dma("gpsimd", wo[:, h * DM:(h + 1) * DM], wo_dram[h, :, :],
                  outs_accum=[wo_b])
        att_st = [sb(f"{tag}_ast{i}", [128, H * 128], BF16) for i in range(2)]
        att_st_b = [R.buf(f"{tag}_ast{i}") for i in range(2)]
        yps = [ps(f"{tag}_yps{i}", [128, 512]) for i in range(4)]
        yps_b = [R.buf(f"{tag}_yps{i}") for i in range(4)]
        ybuf = sb(tag + "_ybuf", [128, DM])
        ybuf_b = R.buf(tag + "_ybuf")
        xstage = sb(tag + "_xst", [128, DM])
        xstage_b = R.buf(tag + "_xst")
        ybf = sb(tag + "_ybf", [128, DM], BF16)
        ybf_b = R.buf(tag + "_ybf")
        lnsc = _mk_ln_scratch(R, tag + "_ln", alloc=sb)
        for st in range(NST):
            pbuf = att_st[st % 2]
            pbuf_b = att_st_b[st % 2]
            R.dma("sync", pbuf[:, :], zh_hbm[st, :, :],
                  ins=list(zh_hbm_b), outs=[pbuf_b])
            R.dma("sync", xstage[:, :], xtok_dram[st * 128:(st + 1) * 128, :],
                  outs=[xstage_b])
            for oc in range(4):
                R.mm(yps[oc][:, :],
                     [(pbuf[:, h * 128:(h + 1) * 128],
                       wo[:, h * DM + oc * 512: h * DM + oc * 512 + 512])
                      for h in range(H)],
                     ins=[pbuf_b, wo_b], outs=[yps_b[oc]])
                R.op("vector", lambda e, _o=oc: e.tensor_tensor(
                    ybuf[:, _o * 512:(_o + 1) * 512],
                    xstage[:, _o * 512:(_o + 1) * 512], yps[_o][:, :], ALU.add),
                    ins=[xstage_b, yps_b[oc]], outs=[ybuf_b], risky=False)
            _ln_tile(R, C, lnsc, ybuf[:, :], ybuf_b)
            _store_tile(R, C, st, ybuf[:, :], ybuf_b, out_dram, last_phase, ybf, ybf_b)



def emit_phaseC(R, C, dr, xtok_dram, out_dram, last_phase=False):
    nc = C.nc
    SCALE = float(HD) ** -0.5
    z_hbm = nc.dram_tensor("a1_z_scr", [NST, 128, H * 128], BF16, kind="Internal")
    z_hbm_b = [R.buf(f"a1_z{h}") for h in range(H)]

    with ExitStack() as sco:
        sb = lambda nm, shp, dt=F32: sco.enter_context(nc.sbuf_tensor(nm, shp, dt))
        ps = lambda nm, shp, dt=F32: sco.enter_context(nc.psum_tensor(nm, shp, dt))
        alcols = sb("c_alc", [128, H * 16])
        alcols_b = R.buf("c_alc")
        R.dma("gpsimd", alcols[:, :], dr["c_alibi_cols"][:, :], outs=[alcols_b])
        nslope = sb("c_nsl", [1, S])
        nslope_b = R.buf("c_nsl")
        camask = sb("c_cam", [128, 4 * 512])
        camask_b = R.buf("c_cam")
        R.dma("gpsimd", camask[:, :], dr["c_camask"][:, :], outs=[camask_b])

        wq = [sb(f"c_wq{i}", [128, NDC * HD], BF16) for i in range(2)]
        wq_b = [R.buf(f"c_wq{i}") for i in range(2)]
        wk = [sb(f"c_wk{i}", [128, NDC * HD], BF16) for i in range(2)]
        wk_b = [R.buf(f"c_wk{i}") for i in range(2)]
        wv = [sb(f"c_wv{i}", [128, NDC * HD], BF16) for i in range(2)]
        wv_b = [R.buf(f"c_wv{i}") for i in range(2)]
        qT = sb("c_qT", [128, S], BF16)
        qT_b = R.buf("c_qT")
        kT = sb("c_kT", [128, S], BF16)
        kT_b = R.buf("c_kT")
        v_tok = sb("c_vtok", [128, S], BF16)
        v_tok_b = R.buf("c_vtok")
        zT = sb("c_zT", [128, S], BF16)
        zT_b = R.buf("c_zT")
        sqk = sb("c_sqk", [128, S], BF16)
        sqk_b = R.buf("c_sqk")
        qn2 = sb("c_qn2", [1, 2 * S])
        qn2_b = R.buf("c_qn2")
        chrow = sb("c_chrow", [1, 8])
        chrow_b = R.buf("c_chrow")
        acrow = sb("c_acrow", [1, S])
        acrow_b = R.buf("c_acrow")
        acbc = sb("c_acbc", [128, 5 * 512])
        acbc_b = R.buf("c_acbc")
        rec_b512 = sb("c_recb", [128, 512])
        rec_b512_b = R.buf("c_recb")
        clp = [sb(f"c_clp{i}", [128, 512]) for i in range(2)]
        clp_b = [R.buf(f"c_clp{i}") for i in range(2)]
        clp2 = [sb(f"c_clp2{i}", [128, 512]) for i in range(2)]
        clp2_b = [R.buf(f"c_clp2{i}") for i in range(2)]
        et = [sb(f"c_et{i}", [128, 512], BF16) for i in range(2)]
        et_b = [R.buf(f"c_et{i}") for i in range(2)]
        srow = sb("c_srow", [1, 512])
        srow_b = R.buf("c_srow")
        hsc = _mk_hilo(R, sb, "c_hsc", 1, 512)
        pA = ps("c_pA", [128, 512])
        pA_b = R.buf("c_pA")
        pB = ps("c_pB", [128, 512])
        pB_b = R.buf("c_pB")
        pC = ps("c_pC", [128, 512])
        pC_b = R.buf("c_pC")
        pR = ps("c_pR", [1, 512])
        pR_b = R.buf("c_pR")
        pD = ps("c_pD", [128, 512])
        pD_b = R.buf("c_pD")

        for h in range(H):
            p = h % 2
            R.dma("gpsimd", wq[p][:, :], dr["a1_q"][h, :, :], outs=[wq_b[p]])
            R.dma("gpsimd", wk[p][:, :], dr["a1_k"][h, :, :], outs=[wk_b[p]])
            R.dma("gpsimd", wv[p][:, :], dr["a1_v"][h, :, :], outs=[wv_b[p]])
            R.dma("gpsimd", nslope[0:1, :], dr["c_negslope"][h:h + 1, :],
                  outs=[nslope_b])
            for n in range(NST):
                R.mm(pC[:, 0:128],
                     [(C.xTa(dc)[:, n * 128:(n + 1) * 128],
                       wv[p][:, dc * HD:(dc + 1) * HD]) for dc in range(NDC)],
                     ins=[wv_b[p]] + C.xT_b, outs=[pC_b])
                R.op("scalar", lambda e, _n=n: e.activation(
                    v_tok[:, _n * 128:(_n + 1) * 128], pC[:, 0:128], AF.Copy),
                    ins=[pC_b], outs=[v_tok_b], risky=False)
            for sc_i in range(4):
                c0 = sc_i * 512
                for (w_t, wbuf, pp, ppb, dstT, dstT_b, scl) in (
                        (wq[p], wq_b[p], pA, pA_b, qT, qT_b, SCALE),
                        (wk[p], wk_b[p], pB, pB_b, kT, kT_b, None)):
                    R.mm(pp[:, :], [(w_t[:, dc * HD:(dc + 1) * HD],
                                     C.xTa(dc)[:, c0:c0 + 512]) for dc in range(NDC)],
                         ins=[wbuf] + C.xT_b, outs=[ppb])
                    if scl is None:
                        R.op("vector", lambda e, _p=pp, _d=dstT, _c=c0: e.tensor_copy(
                            _d[:, _c:_c + 512], _p[:, :]),
                            ins=[ppb], outs=[dstT_b], risky=False)
                    else:
                        R.op("vector", lambda e, _p=pp, _d=dstT, _c=c0: e.tensor_scalar(
                            _d[:, _c:_c + 512], _p[:, :], SCALE, None, ALU.mult),
                            ins=[ppb], outs=[dstT_b], risky=False)
            R.op("scalar", lambda e: e.activation(sqk[:, :], qT[:, :], AF.Square),
                 ins=[qT_b], outs=[sqk_b], risky=False)
            for sc_i in range(4):
                R.mm(pR[0:1, :], [(C.ones_col_b[:, 0:1],
                                   sqk[:, sc_i * 512:(sc_i + 1) * 512])],
                     ins=[sqk_b] + C.const_b, outs=[pR_b])
                R.op("vector", lambda e, _c=sc_i * 512: e.tensor_copy(
                    qn2[0:1, _c:_c + 512], pR[0:1, :]),
                    ins=[pR_b], outs=[qn2_b], risky=False)
            R.op("scalar", lambda e: e.activation(sqk[:, :], kT[:, :], AF.Square),
                 ins=[kT_b], outs=[sqk_b], risky=False)
            for sc_i in range(4):
                R.mm(pR[0:1, :], [(C.ones_col_b[:, 0:1],
                                   sqk[:, sc_i * 512:(sc_i + 1) * 512])],
                     ins=[sqk_b] + C.const_b, outs=[pR_b])
                R.op("vector", lambda e, _c=S + sc_i * 512: e.tensor_copy(
                    qn2[0:1, _c:_c + 512], pR[0:1, :]),
                    ins=[pR_b], outs=[qn2_b], risky=False)
            R.op("vector", lambda e: e.tensor_reduce(chrow[0:1, 0:1], qn2[0:1, 0:S],
                                                     AX.X, ALU.max),
                 ins=[qn2_b], outs=[chrow_b])
            R.op("vector", lambda e: e.tensor_reduce(chrow[0:1, 1:2], qn2[0:1, S:2 * S],
                                                     AX.X, ALU.max),
                 ins=[qn2_b], outs=[chrow_b])
            R.op("vector", lambda e: e.tensor_tensor(chrow[0:1, 2:3], chrow[0:1, 0:1],
                                                     chrow[0:1, 1:2], ALU.mult),
                 ins=[chrow_b], outs=[chrow_b])
            R.op("scalar", lambda e: e.activation(chrow[0:1, 3:4], chrow[0:1, 2:3],
                                                  AF.Ln),
                 ins=[chrow_b], outs=[chrow_b])
            R.op("scalar", lambda e: e.activation(chrow[0:1, 4:5], chrow[0:1, 3:4],
                                                  AF.Exp, scale=0.5),
                 ins=[chrow_b], outs=[chrow_b])
            R.op("vector", lambda e: e.tensor_scalar(chrow[0:1, 5:6], chrow[0:1, 4:5],
                                                     1024.0, None, ALU.min),
                 ins=[chrow_b], outs=[chrow_b])
            R.op("vector", lambda e: e.tensor_scalar(
                acrow[0:1, :], nslope[0:1, :], chrow[0:1, 5:6],
                None, ALU.subtract),
                ins=[nslope_b, chrow_b], outs=[acrow_b])

            for it in range(4):
                _bcast_row(R, C, hsc, pC[:, :], pC_b,
                           acrow[0:1, it * 512:(it + 1) * 512], [acrow_b], 512)
                R.op("vector", lambda e: e.tensor_copy(acbc[:, 0:512], pC[:, :]),
                     ins=[pC_b], outs=[acbc_b], risky=False)
                for r in range(4):
                    R.op("gpsimd", lambda e, _r=r: e.tensor_tensor(
                        acbc[:, (_r + 1) * 512:(_r + 2) * 512], acbc[:, 0:512],
                        camask[:, _r * 512:(_r + 1) * 512], ALU.add),
                        ins=[acbc_b, camask_b], outs=[acbc_b], risky=False)
                njt = 4 * it + 4
                for jt in range(njt + 1):
                    if jt < njt:
                        pSC, pSCb = (pA, pA_b) if jt % 2 == 0 else (pD, pD_b)
                        cl, clb = clp[jt % 2], clp_b[jt % 2]
                        c2, c2b = clp2[jt % 2], clp2_b[jt % 2]
                        eb, ebb = et[jt % 2], et_b[jt % 2]
                        R.mm(pSC[:, :], [(kT[:, jt * 128:(jt + 1) * 128],
                                          qT[:, it * 512:(it + 1) * 512])],
                             ins=[kT_b, qT_b], outs=[pSCb])
                        R.op("vector", lambda e, _p=pSC, _c=cl: e.tensor_scalar(
                            _c[:, :], _p[:, :], -1024.0, 1024.0, ALU.max, ALU.min),
                            ins=[pSCb], outs=[clb], risky=False)
                        r = jt - 4 * it
                        ac_off = (r + 1) * 512 if r >= 0 else 0
                        R.op("gpsimd", lambda e, _o=ac_off, _c=cl, _c2=c2:
                             e.tensor_tensor(_c2[:, :], _c[:, :],
                                             acbc[:, _o:_o + 512], ALU.add),
                             ins=[clb, acbc_b], outs=[c2b], risky=False)
                        R.op("scalar", lambda e, _h=h, _jt=jt, _e=eb, _c2=c2:
                             e.activation(_e[:, :], _c2[:, :], AF.Exp,
                                          bias=alcols[:, _h * 16 + _jt:
                                                      _h * 16 + _jt + 1]),
                             ins=[c2b, alcols_b], outs=[ebb], risky=False)
                    if jt > 0:
                        j0 = jt - 1
                        eb0, ebb0 = et[j0 % 2], et_b[j0 % 2]
                        R.mm(pB[:, :], [(v_tok[:, j0 * 128:(j0 + 1) * 128],
                                         eb0[:, :])],
                             ins=[v_tok_b, ebb0], outs=[pB_b],
                             first=(j0 == 0), last=(j0 == njt - 1))
                        R.mm(pR[0:1, :], [(C.ones_col_b[:, 0:1], eb0[:, :])],
                             ins=[ebb0] + C.const_b, outs=[pR_b],
                             first=(j0 == 0), last=(j0 == njt - 1))
                R.op("vector", lambda e: e.reciprocal(srow[0:1, :], pR[0:1, :]),
                     ins=[pR_b], outs=[srow_b])
                _bcast_row(R, C, hsc, pC[:, :], pC_b, srow[0:1, :], [srow_b], 512)
                R.op("vector", lambda e: e.tensor_copy(rec_b512[:, :], pC[:, :]),
                     ins=[pC_b], outs=[rec_b512_b], risky=False)
                R.op("vector", lambda e, _it=it: e.tensor_tensor(
                    zT[:, _it * 512:(_it + 1) * 512], pB[:, :], rec_b512[:, :],
                    ALU.mult),
                    ins=[pB_b, rec_b512_b], outs=[zT_b], risky=False)
            for g4 in range(4):
                R.dma("sync",
                      z_hbm[g4 * 4:(g4 + 1) * 4, :, h * 128:(h + 1) * 128]
                      .rearrange("s p e -> p s e"),
                      zT[:, g4 * 512:(g4 + 1) * 512]
                      .rearrange("p (s e) -> p s e", s=4),
                      ins=[zT_b], outs_accum=[z_hbm_b[h]])

    R.barrier()
    _oproj_epilogue(R, C, "c", z_hbm, z_hbm_b, dr["a1_wo"], xtok_dram,
                    out_dram, last_phase)



_COMMON_DRAM = {
    "a0_q": [H, 128, NDC * HD], "a0_k": [H, 128, NDC * HD], "a0_v": [H, 128, NDC * HD],
    "a0_wo": [H, HD, DM], "a0_mem": [H, HD, NM * HD], "a0_norms": [H, HD, NM],
    "a0_normsf": [NM * H, HD], "a0_gate": [1, H],
    "f0_gwT": [128, NDC * NE], "f0_wg": [NE, NIC, 128, NDC * 128],
    "f0_wu": [NE, NIC, 128, NDC * 128], "f0_wd": [NE, 4, NIC, 128, 512],
    "f0_swg": [NSC, 128, NDC * 128], "f0_swu": [NSC, 128, NDC * 128],
    "f0_swd": [4, NSC, 128, 512], "f0_sgw": [128, NDC],
    "a1_q": [H, 128, NDC * HD], "a1_k": [H, 128, NDC * HD], "a1_v": [H, 128, NDC * HD],
    "a1_wo": [H, HD, DM],
    "f1_wfc": [NHC, 128, NDC * 128], "f1_wpr": [NHC, 128, DM],
    "c_alibi_cols": [128, H * 16], "c_negslope": [H, S], "c_trilA": [128, 128],
    "c_camask": [128, 4 * 512], "c_msel": [NM * H, NM],
}

_PHASE_KEYS = {
    "A": ["a0_q", "a0_k", "a0_v", "a0_wo", "a0_mem", "a0_norms", "a0_normsf",
          "a0_gate", "c_trilA", "c_msel"],
    "B": ["f0_gwT", "f0_wg", "f0_wu", "f0_wd", "f0_swg", "f0_swu", "f0_swd",
          "f0_sgw"],
    "C": ["a1_q", "a1_k", "a1_v", "a1_wo", "c_alibi_cols", "c_negslope", "c_camask"],
    "D": ["f1_wfc", "f1_wpr"],
}

_EMITTERS = {}


def _build_program(phases):
    nc = bass.Bass("TRN2", target_bir_lowering=False)
    keys = sorted({k for p in phases for k in _PHASE_KEYS[p]})
    dr = {k: nc.dram_tensor(k, _COMMON_DRAM[k], F32, kind="ExternalInput")
          for k in keys}
    x_tok = nc.dram_tensor("x_tok", [S, DM], F32, kind="ExternalInput")
    x_feat = nc.dram_tensor("x_feat", [DM, S], F32, kind="ExternalInput")
    out = nc.dram_tensor("out", [S, DM], F32, kind="ExternalOutput")
    scratch = [nc.dram_tensor(f"xscr{i}", [S, DM], F32, kind="Internal")
               for i in range(max(0, len(phases) - 1))]
    scratch_b = [None] * len(scratch)

    R = Rec(nc)
    C = _setup_common(nc, R, x_feat[:, :])
    cur_in = x_tok
    cur_in_b = None
    tern = None
    if "D" in phases and phases[0] != "D":
        tern = emit_phaseD_tern(R, C, dr)
    for i, p in enumerate(phases):
        last = (i == len(phases) - 1)
        dst = out if last else scratch[i]
        dst_b = None if last else R.buf(f"xscr{i}")
        if p == "D":
            emit_phaseD(R, C, dr, cur_in, dst, last_phase=last, tern=tern)
            tern = None
        else:
            _EMITTERS[p](R, C, dr, cur_in, dst, last_phase=last,
                         xin_buf=cur_in_b, xout_buf=dst_b)
        if tern is not None and i == 0:
            tern["es"].close()
        if not last:
            R.barrier()
        cur_in = dst
        cur_in_b = dst_b
    R.barrier()
    o = _Op("sync", "eng:sync", lambda e: None, 1)
    o.waits = list(R._btoks)
    R.ops.append(o)
    R.emit()
    R.es.close()
    return nc


def _wrap_emit(emitfn):
    def w(R, C, dr, xin, xout, last_phase, xin_buf=None, xout_buf=None):
        xin_ap = xin
        emitfn(R, C, dr, xin, xout, last_phase=last_phase)
    return w


_EMITTERS["D"] = _wrap_emit(emit_phaseD)
_EMITTERS["B"] = _wrap_emit(emit_phaseB)
_EMITTERS["A"] = _wrap_emit(emit_phaseA)
_EMITTERS["C"] = _wrap_emit(emit_phaseC)


def _run(phases, inputs):
    cm = _prep_common(inputs)
    x = np.ascontiguousarray(np.asarray(inputs["L0_attn_hidden_states"],
                                        dtype=np.float32))
    nc = _build_program(phases)
    keys = sorted({k for p in phases for k in _PHASE_KEYS[p]})
    in_maps = []
    for c in range(8):
        m = {k: cm[k] for k in keys}
        m["x_tok"] = np.ascontiguousarray(x[c])
        m["x_feat"] = np.ascontiguousarray(x[c].T)
        in_maps.append(m)
    res = run_bass_kernel_spmd(nc, in_maps, core_ids=list(range(8)))
    return np.stack([res.results[c]["out"] for c in range(8)], axis=0)


def _run_chain(groups, inputs):
    cm = _prep_common(inputs)
    x = np.ascontiguousarray(np.asarray(inputs["L0_attn_hidden_states"],
                                        dtype=np.float32))
    cur = [np.ascontiguousarray(x[c]) for c in range(8)]
    for g in groups:
        nc = _build_program(g)
        keys = sorted({k for p in g for k in _PHASE_KEYS[p]})
        in_maps = []
        for c in range(8):
            m = {k: cm[k] for k in keys}
            m["x_tok"] = np.ascontiguousarray(cur[c])
            m["x_feat"] = np.ascontiguousarray(cur[c].T)
            in_maps.append(m)
        res = run_bass_kernel_spmd(nc, in_maps, core_ids=list(range(8)))
        cur = [np.asarray(res.results[c]["out"]) for c in range(8)]
    return np.stack(cur, axis=0)


def kernel(**inputs):
    return _run_chain([("A", "B", "C", "D")], inputs)
